# version 02
# baseline (speedup 1.0000x reference)
import numpy as np
import contextlib
import concourse.bass as bass
import concourse.mybir as mybir
from concourse.bass_utils import run_bass_kernel_spmd
from concourse.bass_core_rust import IndirectOffsetOnAxis

E = 1024
EPS = 1e-5

F32 = mybir.dt.float32
BF16 = mybir.dt.bfloat16
I32 = mybir.dt.int32
U32 = mybir.dt.uint32
F32R = mybir.dt.float32r
AF = mybir.ActivationFunctionType
ALU = mybir.AluOpType
AX = mybir.AxisListType

ENGINES = ("sync", "tensor", "vector", "scalar", "gpsimd")


class Plan:
    def __init__(self, nc, es):
        self.nc = nc
        self.items = {e: [] for e in ENGINES}
        self.cum = {e: 0 for e in ENGINES}
        self.sems = {e: es.enter_context(nc.semaphore(f"s_{e}")) for e in ENGINES}
        self.writer = {}
        self.readers = {}
        self.waited = {c: {p: 0 for p in ENGINES} for c in ENGINES}
        self.nwaits = 0

    def _dep(self, waits, eng, prod, cdma):
        pe, pc, pdma, phaz = prod
        if pe == eng == "tensor" and not pdma and not cdma:
            return
        if self.waited[eng][pe] >= pc:
            return
        self.waited[eng][pe] = pc
        waits.append((pe, pc))
        self.nwaits += 1

    def op(self, eng, fn, reads=(), writes=(), dma=False, hazard=False):
        waits = []
        for b in reads:
            w = self.writer.get(b)
            if w:
                self._dep(waits, eng, w, dma)
        for b in writes:
            w = self.writer.get(b)
            if w:
                self._dep(waits, eng, w, dma)
            for r in self.readers.get(b, ()):
                self._dep(waits, eng, r, dma)
        inc = 16 if dma else 1
        self.cum[eng] += inc
        cumafter = self.cum[eng]
        rec = (eng, cumafter, dma, hazard)
        for b in writes:
            self.writer[b] = rec
            self.readers[b] = []
        for b in reads:
            self.readers.setdefault(b, []).append(rec)
        self.items[eng].append((tuple(waits), fn, inc))
        return rec

    def raw_wait(self, eng, waits):
        self.items[eng].append((tuple(waits), None, 0))

    def barrier(self):
        snap = dict(self.cum)
        for e in ENGINES:
            waits = []
            for p in ENGINES:
                if p != e and snap[p] > self.waited[e][p]:
                    self.waited[e][p] = snap[p]
                    waits.append((p, snap[p]))
            if waits:
                self.raw_wait(e, waits)

    def final_wait(self, eng):
        snap = dict(self.cum)
        waits = tuple((p, snap[p]) for p in ENGINES if p != eng and snap[p] > 0)
        self.raw_wait(eng, waits)

    def emit_into(self, block):
        for name in ENGINES:
            items = self.items[name]
            mysem = self.sems[name]
            sems = self.sems

            def runner(eng, items=items, mysem=mysem, sems=sems):
                for waits, fn, inc in items:
                    for pe, pc in waits:
                        eng.wait_ge(sems[pe], pc)
                    if fn is not None:
                        fn(eng).then_inc(mysem, inc)

            getattr(block, name)(runner)




def build_l1(NT=32, debug_tile=None):
    TOK = NT * 128
    nc = bass.Bass("TRN2", target_bir_lowering=False)

    xT = nc.dram_tensor("xT", [E, TOK], F32, kind="ExternalInput")
    wq = nc.dram_tensor("wq", [E, E], F32, kind="ExternalInput")
    wk = nc.dram_tensor("wk", [E, E], F32, kind="ExternalInput")
    wv = nc.dram_tensor("wv", [E, E], F32, kind="ExternalInput")
    wo = nc.dram_tensor("wo", [E, E], F32, kind="ExternalInput")
    wg1 = nc.dram_tensor("wg1", [E, 64], F32, kind="ExternalInput")
    wg2 = nc.dram_tensor("wg2", [64, E], F32, kind="ExternalInput")
    gam = nc.dram_tensor("gam", [128, 8], F32, kind="ExternalInput")
    bet = nc.dram_tensor("bet", [128, 8], F32, kind="ExternalInput")
    wgr = nc.dram_tensor("wgr", [E, 16], F32, kind="ExternalInput")
    bgr = nc.dram_tensor("bgr", [1, 16], F32, kind="ExternalInput")
    cosr = nc.dram_tensor("cosr", [128, TOK], F32, kind="ExternalInput")
    sinr = nc.dram_tensor("sinr", [128, TOK], F32, kind="ExternalInput")
    triu_d = nc.dram_tensor("triu_d", [128, 128], F32, kind="ExternalInput")
    ident_d = nc.dram_tensor("ident_d", [128, 128], F32, kind="ExternalInput")
    iota16_d = nc.dram_tensor("iota16_d", [128, 16], F32, kind="ExternalInput")
    s0_in = nc.dram_tensor("s0_in", [128, 256], F32, kind="ExternalInput")
    s1_in = nc.dram_tensor("s1_in", [128, 256], F32, kind="ExternalInput")

    y_out = nc.dram_tensor("y_out", [TOK, E], F32, kind="ExternalOutput")
    m_out = nc.dram_tensor("m_out", [TOK, 16], F32, kind="ExternalOutput")
    dbg_names = ["d_u1", "d_qr", "d_kr", "d_og", "d_u2"]
    dbg16 = ["d_lgt", "d_sc", "d_mk1"]
    dbg = {}
    if debug_tile is not None:
        for nm in dbg_names:
            dbg[nm] = nc.dram_tensor(nm, [128, 8 * 128], F32, kind="ExternalOutput")
        for nm in dbg16:
            dbg[nm] = nc.dram_tensor(nm, [128, 16], F32, kind="ExternalOutput")

    es = contextlib.ExitStack()
    with es:
        sbuf = lambda name, shape: es.enter_context(nc.sbuf_tensor(name, shape, F32))
        psum = lambda name: es.enter_context(nc.psum_tensor(name, [128, 512], F32))

        wq_s = sbuf("wq_s", [128, 8 * E]); wk_s = sbuf("wk_s", [128, 8 * E])
        wv_s = sbuf("wv_s", [128, 8 * E]); wo_s = sbuf("wo_s", [128, 8 * E])
        wg1_s = sbuf("wg1_s", [128, 8 * 64]); wg2_s = sbuf("wg2_s", [64, E])
        wgr_s = sbuf("wgr_s", [128, 8 * 16]); bgr_s = sbuf("bgr_s", [1, 16])
        gam_s = sbuf("gam_s", [128, 8]); bet_s = sbuf("bet_s", [128, 8])
        triu_s = sbuf("triu_s", [128, 128]); id_s = sbuf("id_s", [128, 128])
        io16 = sbuf("io16", [128, 16]); c16 = sbuf("c16", [128, 16])
        ones1 = sbuf("ones1", [1, 128]); onesc = sbuf("onesc", [128, 1])
        xf = [sbuf(f"xf{j}", [128, 8 * 128]) for j in range(2)]
        csb = [sbuf(f"csb{j}", [128, 128]) for j in range(2)]
        snb = [sbuf(f"snb{j}", [128, 128]) for j in range(2)]
        u1 = sbuf("u1", [128, 8 * 128])
        sq = sbuf("sq", [128, 8 * 128])
        qs = sbuf("qs", [128, 8 * 128]); qr = sbuf("qr", [128, 8 * 128])
        ks = sbuf("ks", [128, 8 * 128]); kr = sbuf("kr", [128, 8 * 128])
        rtv = sbuf("rtv", [128, 128]); rtg = sbuf("rtg", [128, 128])
        ktk = sbuf("ktk", [128, 8 * 128]); vtk = sbuf("vtk", [128, 8 * 128])
        t1s = sbuf("t1s", [64, 128]); gsg = sbuf("gsg", [128, 8 * 128])
        etm = sbuf("etm", [128, 128])
        og, KOG = qs, "qs"
        l2o, KL2 = ks, "ks"
        wos, KWOS = u1, "u1"
        ys = sbuf("ys", [128, 8 * 128])
        u2f, KU2 = gsg, "gsg"
        ytk, KYTK = sq, "sq"
        s0 = sbuf("s0", [128, 4 * 64]); s1 = sbuf("s1", [128, 4 * 64])
        cvals = sbuf("cvals", [128, 2])
        mur = sbuf("mur", [1, 128]); msq = sbuf("msq", [1, 128])
        vrr = sbuf("vrr", [1, 128]); rsd = sbuf("rsd", [1, 128])
        lgt = sbuf("lgt", [128, 16]); sc = sbuf("sc", [128, 16])
        sm1 = sbuf("sm1", [128, 1]); sm2 = sbuf("sm2", [128, 1])
        t16a = sbuf("t16a", [128, 16]); t16b = sbuf("t16b", [128, 16])
        mk1 = sbuf("mk1", [128, 16]); msk = sbuf("msk", [128, 16])
        m16i = es.enter_context(nc.sbuf_tensor("m16i", [128, 16], mybir.dt.int32))

        pA = psum("pA"); pB = psum("pB"); pC = psum("pC"); pD = psum("pD")
        pE = psum("pE"); pF = psum("pF"); pG = psum("pG"); pH = psum("pH")
        PK = {id(p): n for p, n in [(pA, "pA"), (pB, "pB"), (pC, "pC"), (pD, "pD"),
                                    (pE, "pE"), (pF, "pF"), (pG, "pG"), (pH, "pH")]}

        P = Plan(nc, es)
        blk = es.enter_context(nc.Block())

        def key(b):
            return b.name if hasattr(b, "name") else str(b)

        K = lambda b: PK.get(id(b), getattr(b, "name", str(b)))
        CH = lambda b, c: b[:, c * 128:(c + 1) * 128]

        def dma(out_ap, in_ap, rd, wr, eng="sync"):
            P.op(eng, lambda e, o=out_ap, i=in_ap: e.dma_start(out=o, in_=i),
                 reads=rd, writes=wr, dma=True)

        rr = lambda t: t.ap().rearrange("(c p) o -> p c o", p=128)
        dma(wq_s[:], rr(wq), ["wq"], ["wq_s"])
        dma(wk_s[:], rr(wk), ["wk"], ["wk_s"])
        dma(wv_s[:], rr(wv), ["wv"], ["wv_s"])
        dma(wo_s[:], rr(wo), ["wo"], ["wo_s"])
        dma(wg1_s[:], rr(wg1), ["wg1"], ["wg1_s"])
        dma(wg2_s[:], wg2[:, :], ["wg2"], ["wg2_s"])
        dma(wgr_s[:], rr(wgr), ["wgr"], ["wgr_s"])
        dma(bgr_s[:], bgr[:, :], ["bgr"], ["bgr_s"])
        dma(gam_s[:], gam[:, :], ["gam"], ["gam_s"])
        dma(bet_s[:], bet[:, :], ["bet"], ["bet_s"])
        dma(triu_s[:], triu_d[:, :], ["triu_d"], ["triu_s"])
        dma(id_s[:], ident_d[:, :], ["ident_d"], ["id_s"])
        dma(io16[:], iota16_d[:, :], ["iota16_d"], ["io16"])
        xtr = xT.ap().rearrange("(c p) n -> p c n", p=128)
        dma(xf[0][:], xtr[:, :, 0:128], ["xT"], ["xf0"])
        dma(csb[0][:], cosr[:, 0:128], ["cosr"], ["csb0"])
        dma(snb[0][:], sinr[:, 0:128], ["sinr"], ["snb0"])

        dma(s0[:], s0_in[:, :], ["s0_in"], ["s0"])
        dma(s1[:], s1_in[:, :], ["s1_in"], ["s1"])
        P.op("vector", lambda e: e.memset(ones1[:], 1.0), writes=["ones1"])
        P.op("vector", lambda e: e.memset(onesc[:], 1.0), writes=["onesc"])
        P.op("vector", lambda e: e.memset(c16[:], 16.0), writes=["c16"])
        P.op("vector", lambda e: e.memset(cvals[:, 0:1], 1e-5), writes=["cvals"])
        P.op("vector", lambda e: e.memset(cvals[:, 1:2], 1.0 / 1024.0), writes=["cvals"])
        nc.const_aps.register(F32, 1e-5, cvals[:, 0:1])
        nc.const_aps.register(F32, 1.0 / 1024.0, cvals[:, 1:2])

        def mm(out_ap, lhsT, rhs, rd, pbank, start, stop, tile_position=None):
            P.op("tensor",
                 lambda e, o=out_ap, l=lhsT, r=rhs, s=start, t=stop, tp=tile_position:
                 e.matmul(o, l, r, start=s, stop=t, tile_position=tp),
                 reads=rd, writes=[pbank])

        def transp(out_ap, in_ap, rd, pbank):
            P.op("tensor", lambda e, o=out_ap, i=in_ap: e.transpose(o, i, id_s[:]),
                 reads=list(rd) + ["id_s"], writes=[pbank])

        def tt(eng, out_ap, a, b, op, rd, wr):
            P.op(eng, lambda e, o=out_ap, x=a, y=b, p=op: e.tensor_tensor(o, x, y, p),
                 reads=rd, writes=wr)

        def ts(eng, out_ap, a, s1v, s2v, op0, op1, rd, wr):
            if s2v is None and op1 is None:
                P.op(eng, lambda e, o=out_ap, x=a, s=s1v, p=op0:
                     e.tensor_scalar(o, x, s, None, p), reads=rd, writes=wr)
            else:
                P.op(eng, lambda e, o=out_ap, x=a, s=s1v, s2=s2v, p=op0, p2=op1:
                     e.tensor_scalar(o, x, s, s2, p, p2), reads=rd, writes=wr)

        def stt(eng, out_ap, in0, scal, in1, op0, op1, rd, wr):
            P.op(eng, lambda e, o=out_ap, a=in0, s=scal, b=in1, p0=op0, p1=op1:
                 e.scalar_tensor_tensor(o, a, s, b, p0, p1), reads=rd, writes=wr)

        def act(out_ap, in_ap, func, rd, wr, bias=0.0, scale=1.0):
            P.op("scalar", lambda e, o=out_ap, i=in_ap, f=func, b=bias, s=scale:
                 e.activation(o, i, f, bias=b, scale=s), reads=rd, writes=wr)

        def vcopy(eng, out_ap, in_ap, rd, wr):
            P.op(eng, lambda e, o=out_ap, i=in_ap: e.tensor_copy(o, i),
                 reads=rd, writes=wr)

        def stats(src, srckey):
            tt("vector", sq[:], src[:], src[:], ALU.mult, [srckey], ["sq"])
            for c in range(8):
                mm(pE[0:1, 0:128], onesc[:], CH(src, c), [srckey, "onesc"], "pE",
                   start=(c == 0), stop=(c == 7))
            for c in range(8):
                mm(pE[0:1, 128:256], onesc[:], CH(sq, c), ["sq", "onesc"], "pE",
                   start=(c == 0), stop=(c == 7))
            act(mur[:], pE[0:1, 0:128], AF.Copy, ["pE", "cvals"], ["mur"], scale=1.0 / 1024.0)
            tt("vector", msq[:], mur[:], mur[:], ALU.mult, ["mur"], ["msq"])
            stt("vector", vrr[:], pE[0:1, 128:256], 1.0 / 1024.0, msq[:],
                ALU.mult, ALU.subtract, ["pE", "msq"], ["vrr"])
            act(rsd[:], vrr[:], AF.Sqrt, ["vrr", "cvals"], ["rsd"], bias=EPS)
            P.op("vector", lambda e: e.reciprocal(rsd[:], rsd[:]),
                 reads=["rsd"], writes=["rsd"], hazard=True)
            mm(pE[:, 256:384], ones1[:], mur[:], ["ones1", "mur"], "pE", True, True)
            mm(pE[:, 384:512], ones1[:], rsd[:], ["ones1", "rsd"], "pE", True, True)

        def apply_norm(dst, dstkey, src, srckey, extra_add=None, extrakey=None):
            for c in range(8):
                tt("vector", CH(dst, c), CH(src, c), pE[:, 256:384], ALU.subtract,
                   [srckey, "pE"], [dstkey])
                tt("vector", CH(dst, c), CH(dst, c), pE[:, 384:512], ALU.mult,
                   [dstkey, "pE"], [dstkey])
                if extra_add is not None:
                    tt("vector", CH(dst, c), CH(dst, c), CH(extra_add, c), ALU.add,
                       [dstkey, extrakey], [dstkey])

        def rope(eng, src, srckey, dst, dstkey, tmp, tmpkey, cosb, cosk, sinb, sink):
            for c in range(4):
                s1, s2 = CH(src, c), CH(src, c + 4)
                tt(eng, CH(dst, c), s1, cosb[:], ALU.mult, [srckey, cosk], [dstkey])
                tt(eng, tmp[:], s2, sinb[:], ALU.mult, [srckey, sink], [tmpkey])
                tt(eng, CH(dst, c), CH(dst, c), tmp[:], ALU.subtract,
                   [dstkey, tmpkey], [dstkey])
                tt(eng, CH(dst, c + 4), s1, sinb[:], ALU.mult, [srckey, sink], [dstkey])
                tt(eng, tmp[:], s2, cosb[:], ALU.mult, [srckey, cosk], [tmpkey])
                tt(eng, CH(dst, c + 4), CH(dst, c + 4), tmp[:], ALU.add,
                   [dstkey, tmpkey], [dstkey])

        for i in range(NT):
            xb, cb, sb_ = xf[i % 2], csb[i % 2], snb[i % 2]
            kx, kc_, ks_ = f"xf{i % 2}", f"csb{i % 2}", f"snb{i % 2}"
            if i + 1 < NT:
                nb = (i + 1) % 2
                sl = slice((i + 1) * 128, (i + 2) * 128)
                dma(xf[nb][:], xtr[:, :, sl], ["xT"], [f"xf{nb}"])
                dma(csb[nb][:], cosr[:, sl], ["cosr"], [f"csb{nb}"])
                dma(snb[nb][:], sinr[:, sl], ["sinr"], [f"snb{nb}"])

            stats(xb, kx)
            apply_norm(u1, "u1", xb, kx)

            for oc in range(8):
                pq = pA if oc < 4 else pB
                col = (oc % 4) * 128
                for kc in range(8):
                    mm(pq[:, col:col + 128], wq_s[:, kc * E + oc * 128: kc * E + oc * 128 + 128],
                       CH(u1, kc), ["wq_s", "u1"], K(pq), start=(kc == 0), stop=(kc == 7))
            for oc in range(8):
                pq = pA if oc < 4 else pB
                col = (oc % 4) * 128
                act(CH(qs, oc), pq[:, col:col + 128], AF.Silu, [K(pq)], ["qs"])
            for oc in range(8):
                pq = pA if oc < 4 else pB
                col = (oc % 4) * 128
                for kc in range(8):
                    mm(pq[:, col:col + 128], wk_s[:, kc * E + oc * 128: kc * E + oc * 128 + 128],
                       CH(u1, kc), ["wk_s", "u1"], K(pq), start=(kc == 0), stop=(kc == 7))
            for oc in range(8):
                pq = pA if oc < 4 else pB
                col = (oc % 4) * 128
                act(CH(ks, oc), pq[:, col:col + 128], AF.Silu, [K(pq)], ["ks"])
            for kc in range(8):
                mm(pC[:, :], CH(u1, kc), wv_s[:, kc * E: kc * E + 512],
                   ["wv_s", "u1"], "pC", start=(kc == 0), stop=(kc == 7))
            for kc in range(8):
                mm(pD[:, :], CH(u1, kc), wv_s[:, kc * E + 512: kc * E + 1024],
                   ["wv_s", "u1"], "pD", start=(kc == 0), stop=(kc == 7))
            vcopy("vector", vtk[:, 0:512], pC[:, :], ["pC"], ["vtk"])
            vcopy("vector", vtk[:, 512:1024], pD[:, :], ["pD"], ["vtk"])
            for kc in range(8):
                mm(pH[0:64, 0:128], wg1_s[:, kc * 64: kc * 64 + 64], CH(u1, kc),
                   ["wg1_s", "u1"], "pH", start=(kc == 0), stop=(kc == 7))
            act(t1s[:], pH[0:64, 0:128], AF.Copy, ["pH"], ["t1s"])
            for oc in range(8):
                pg = pC if oc < 4 else pD
                col = (oc % 4) * 128
                mm(pg[:, col:col + 128], wg2_s[:, oc * 128: oc * 128 + 128], t1s[:],
                   ["wg2_s", "t1s"], K(pg), start=True, stop=True)
            for oc in range(8):
                pg = pC if oc < 4 else pD
                col = (oc % 4) * 128
                act(CH(gsg, oc), pg[:, col:col + 128], AF.Sigmoid, [K(pg)], ["gsg"])

            rope("vector", qs, "qs", qr, "qr", rtv, "rtv", cb, kc_, sb_, ks_)
            rope("gpsimd", ks, "ks", kr, "kr", rtg, "rtg", cb, kc_, sb_, ks_)

            for c in range(8):
                pt = pC if c < 4 else pD
                col = (c % 4) * 128
                transp(pt[:, col:col + 128], CH(kr, c), ["kr"], K(pt))
            vcopy("vector", ktk[:, 0:512], pC[:, :], ["pC"], ["ktk"])
            vcopy("vector", ktk[:, 512:1024], pD[:, :], ["pD"], ["ktk"])

            for h in range(16):
                band = 32 * (h % 4)
                hs = h // 4
                pEt = pE if (h % 2 == 0) else pA
                for p_ in range(2):
                    cchunk = 4 * p_ + hs
                    lhs = CH(kr, cchunk)[band:band + 32, :]
                    rhs = CH(qr, cchunk)[band:band + 32, :]
                    mm(pEt[:, 0:128], lhs, rhs, ["kr", "qr"], K(pEt),
                       start=(p_ == 0), stop=(p_ == 1), tile_position=(band, 0))
                tt("vector", etm[:], pEt[:, 0:128], triu_s[:], ALU.mult,
                   [K(pEt), "triu_s"], ["etm"])
                pO = pF if h < 8 else pG
                ocol = 128 * ((h // 2) % 4)
                orow = 64 * (h % 2)
                oap = pO[orow:orow + 64, ocol:ocol + 128]
                mm(oap, vtk[:, 64 * h: 64 * h + 64], etm[:], ["vtk", "etm"], K(pO),
                   start=True, stop=False, tile_position=(0, orow))
                if True:
                    for p_ in range(2):
                        sbuf_t = s0 if p_ == 0 else s1
                        skey = "s0" if p_ == 0 else "s1"
                        lhs = sbuf_t[band:band + 32, 64 * hs: 64 * hs + 64]
                        rhs = CH(qr, 4 * p_ + hs)[band:band + 32, :]
                        mm(oap, lhs, rhs, [skey, "qr"], K(pO),
                           start=False, stop=(p_ == 1), tile_position=(band, orow))
                for p_ in range(2):
                    lhs = ktk[:, 512 * p_ + 32 * h: 512 * p_ + 32 * h + 32]
                    rhs = vtk[:, 64 * h: 64 * h + 64]
                    mm(pH[band:band + 32, 256 * p_ + 64 * hs: 256 * p_ + 64 * hs + 64],
                       lhs, rhs, ["ktk", "vtk"], "pH", start=True, stop=True,
                       tile_position=(0, band))
            tt("vector", s0[:], s0[:], pH[:, 0:256], ALU.add, ["pH", "s0"], ["s0"])
            tt("vector", s1[:], s1[:], pH[:, 256:512], ALU.add, ["pH", "s1"], ["s1"])

            for c in range(8):
                pO = pF if c < 4 else pG
                col = 128 * (c % 4)
                tt("vector", CH(og, c), pO[:, col:col + 128], CH(gsg, c), ALU.mult,
                   [K(pO), "gsg"], [KOG])

            stats(og, KOG)
            apply_norm(l2o, KL2, og, KOG)
            for c in range(8):
                ts("vector", CH(l2o, c), CH(l2o, c), gam_s[:, c:c + 1],
                   bet_s[:, c:c + 1], ALU.mult, ALU.add,
                   [KL2, "gam_s", "bet_s"], [KL2])

            for oc in range(8):
                pw = pA if oc < 4 else pB
                col = (oc % 4) * 128
                for kc in range(8):
                    mm(pw[:, col:col + 128], wo_s[:, kc * E + oc * 128: kc * E + oc * 128 + 128],
                       CH(l2o, kc), ["wo_s", KL2], K(pw), start=(kc == 0), stop=(kc == 7))
            for oc in range(8):
                pw = pA if oc < 4 else pB
                col = (oc % 4) * 128
                vcopy("vector", CH(wos, oc), pw[:, col:col + 128], [K(pw)], [KWOS])

            stats(wos, KWOS)
            apply_norm(ys, "ys", wos, KWOS, extra_add=xb, extrakey=kx)

            stats(ys, "ys")
            apply_norm(u2f, KU2, ys, "ys")

            for c in range(8):
                pt = pC if c < 4 else pD
                col = (c % 4) * 128
                transp(pt[:, col:col + 128], CH(ys, c), ["ys"], K(pt))
            act(ytk[:, 0:512], pC[:, :], AF.Copy, ["pC"], [KYTK])
            act(ytk[:, 512:1024], pD[:, :], AF.Copy, ["pD"], [KYTK])
            dma(y_out[i * 128:(i + 1) * 128, :], ytk[:], [KYTK], ["y_out"])

            for kc in range(8):
                mm(pH[0:128, 256:272], CH(u2f, kc), wgr_s[:, kc * 16: kc * 16 + 16],
                   [KU2, "wgr_s"], "pH", start=(kc == 0), stop=False)
            mm(pH[0:128, 256:272], ones1[:], bgr_s[:], ["ones1", "bgr_s"], "pH",
               start=False, stop=True)
            vcopy("vector", lgt[:], pH[0:128, 256:272], ["pH"], ["lgt"])
            P.op("vector", lambda e: e.reduce_max(sm1[:], lgt[:], axis=AX.X),
                 reads=["lgt"], writes=["sm1"], hazard=True)
            ts("vector", sc[:], lgt[:], sm1[:, 0:1], None, ALU.subtract, None,
               ["lgt", "sm1"], ["sc"])
            act(sc[:], sc[:], AF.Exp, ["sc"], ["sc"])
            P.op("vector", lambda e: e.reduce_sum(sm2[:], sc[:], axis=AX.X),
                 reads=["sc"], writes=["sm2"], hazard=True)
            P.op("vector", lambda e: e.reciprocal(sm2[:], sm2[:]),
                 reads=["sm2"], writes=["sm2"], hazard=True)
            ts("vector", sc[:], sc[:], sm2[:, 0:1], None, ALU.mult, None,
               ["sc", "sm2"], ["sc"])
            P.op("vector", lambda e: e.reduce_max(sm1[:], sc[:], axis=AX.X),
                 reads=["sc"], writes=["sm1"], hazard=True)
            ts("vector", m16i[:], sc[:], sm1[:, 0:1], None, ALU.is_equal, None,
               ["sc", "sm1"], ["m16i"])
            P.op("vector", lambda e: e.select(t16b[:], m16i[:], io16[:], c16[:]),
                 reads=["m16i", "io16", "c16"], writes=["t16b"])
            P.op("vector", lambda e: e.tensor_reduce(sm1[:], t16b[:], axis=AX.X, op=ALU.min),
                 reads=["t16b"], writes=["sm1"], hazard=True)
            ts("vector", mk1[:], io16[:], sm1[:, 0:1], None, ALU.is_equal, None,
               ["io16", "sm1"], ["mk1"])
            stt("vector", t16a[:], mk1[:], -1e30, sc[:], ALU.mult, ALU.add,
                ["mk1", "sc"], ["t16a"])
            P.op("vector", lambda e: e.reduce_max(sm2[:], t16a[:], axis=AX.X),
                 reads=["t16a"], writes=["sm2"], hazard=True)
            ts("vector", m16i[:], t16a[:], sm2[:, 0:1], None, ALU.is_equal, None,
               ["t16a", "sm2"], ["m16i"])
            P.op("vector", lambda e: e.select(t16a[:], m16i[:], io16[:], c16[:]),
                 reads=["m16i", "io16", "c16"], writes=["t16a"])
            P.op("vector", lambda e: e.tensor_reduce(sm2[:], t16a[:], axis=AX.X, op=ALU.min),
                 reads=["t16a"], writes=["sm2"], hazard=True)
            ts("vector", t16b[:], io16[:], sm2[:, 0:1], None, ALU.is_equal, None,
               ["io16", "sm2"], ["t16b"])
            tt("vector", mk1[:], mk1[:], t16b[:], ALU.add, ["mk1", "t16b"], ["mk1"])
            tt("vector", msk[:], sc[:], mk1[:], ALU.mult, ["sc", "mk1"], ["msk"])
            dma(m_out[i * 128:(i + 1) * 128, :], msk[:], ["msk"], ["m_out"])

            if debug_tile is not None and i == debug_tile:
                for nm, buf, bk in [("d_u1", u1, "u1"), ("d_qr", qr, "qr"),
                                    ("d_kr", kr, "kr"), ("d_og", og, KOG),
                                    ("d_u2", u2f, KU2)]:
                    dma(dbg[nm][:, :], buf[:], [bk], [nm])
                for nm, buf, bk in [("d_lgt", lgt, "lgt"), ("d_sc", sc, "sc"),
                                    ("d_mk1", mk1, "mk1")]:
                    dma(dbg[nm][:, :], buf[:], [bk], [nm])

        P.final_wait("sync")
        P.emit_into(blk)
    return nc


BIG = 32768.0



def build_l1a(NT=32):
    TOK = NT * 128
    nc = bass.Bass("TRN2", target_bir_lowering=False)
    xT = nc.dram_tensor("xT", [E, TOK], F32, kind="ExternalInput")
    wk = nc.dram_tensor("wk", [E, E], F32, kind="ExternalInput")
    wv = nc.dram_tensor("wv", [E, E], F32, kind="ExternalInput")
    cosr = nc.dram_tensor("cosr", [128, TOK], F32, kind="ExternalInput")
    sinr = nc.dram_tensor("sinr", [128, TOK], F32, kind="ExternalInput")
    ident_d = nc.dram_tensor("ident_d", [128, 128], F32, kind="ExternalInput")
    s0_out = nc.dram_tensor("s0_out", [128, 256], F32, kind="ExternalOutput")
    s1_out = nc.dram_tensor("s1_out", [128, 256], F32, kind="ExternalOutput")

    es = contextlib.ExitStack()
    with es:
        sbuf = lambda name, shape: es.enter_context(nc.sbuf_tensor(name, shape, F32))
        psum = lambda name: es.enter_context(nc.psum_tensor(name, [128, 512], F32))
        wk_s = sbuf("wk_s", [128, 8 * E]); wv_s = sbuf("wv_s", [128, 8 * E])
        id_s = sbuf("id_s", [128, 128])
        onesc = sbuf("onesc", [128, 1])
        xf = [sbuf(f"xf{j}", [128, 8 * 128]) for j in range(2)]
        csb = [sbuf(f"csb{j}", [128, 128]) for j in range(2)]
        snb = [sbuf(f"snb{j}", [128, 128]) for j in range(2)]
        u1 = sbuf("u1", [128, 8 * 128]); sq = sbuf("sq", [128, 8 * 128])
        ks = sbuf("ks", [128, 8 * 128]); kr = sbuf("kr", [128, 8 * 128])
        rtg = sbuf("rtg", [128, 128])
        ktk = sbuf("ktk", [128, 8 * 128]); vtk = sbuf("vtk", [128, 8 * 128])
        s0 = sbuf("s0", [128, 4 * 64]); s1 = sbuf("s1", [128, 4 * 64])
        mur = sbuf("mur", [1, 128]); msq = sbuf("msq", [1, 128])
        vrr = sbuf("vrr", [1, 128]); rsd = sbuf("rsd", [1, 128])
        cvals = sbuf("cvals", [128, 2])
        ones1 = sbuf("ones1", [1, 128])
        pA = psum("pA"); pB = psum("pB"); pC = psum("pC"); pD = psum("pD")
        pE = psum("pE"); pH = psum("pH")
        PK = {id(p): n for p, n in [(pA, "pA"), (pB, "pB"), (pC, "pC"),
                                    (pD, "pD"), (pE, "pE"), (pH, "pH")]}
        K = lambda b: PK.get(id(b), getattr(b, "name", str(b)))
        CH = lambda b, c: b[:, c * 128:(c + 1) * 128]
        P = Plan(nc, es)
        blk = es.enter_context(nc.Block())
        nc.const_aps.register(F32, EPS, cvals[:, 0:1])
        nc.const_aps.register(F32, 1.0 / 1024.0, cvals[:, 1:2])

        def dma(out_ap, in_ap, rd, wr, eng="sync"):
            P.op(eng, lambda e, o=out_ap, i=in_ap: e.dma_start(out=o, in_=i),
                 reads=rd, writes=wr, dma=True)

        def mm(out_ap, lhsT, rhs, rd, pbank, start, stop, tile_position=None):
            P.op("tensor",
                 lambda e, o=out_ap, l=lhsT, r=rhs, s=start, t=stop, tp=tile_position:
                 e.matmul(o, l, r, start=s, stop=t, tile_position=tp),
                 reads=rd, writes=[pbank])

        def transp(out_ap, in_ap, rd, pbank):
            P.op("tensor", lambda e, o=out_ap, i=in_ap: e.transpose(o, i, id_s[:]),
                 reads=list(rd) + ["id_s"], writes=[pbank])

        def tt(eng, out_ap, a, b, op, rd, wr):
            P.op(eng, lambda e, o=out_ap, x=a, y=b, p=op: e.tensor_tensor(o, x, y, p),
                 reads=rd, writes=wr)

        def stt(eng, out_ap, in0, scal, in1, op0, op1, rd, wr):
            P.op(eng, lambda e, o=out_ap, a=in0, s=scal, b=in1, p0=op0, p1=op1:
                 e.scalar_tensor_tensor(o, a, s, b, p0, p1), reads=rd, writes=wr)

        def act(out_ap, in_ap, func, rd, wr, bias=0.0, scale=1.0):
            P.op("scalar", lambda e, o=out_ap, i=in_ap, f=func, b=bias, s=scale:
                 e.activation(o, i, f, bias=b, scale=s), reads=rd, writes=wr)

        def vcopy(eng, out_ap, in_ap, rd, wr):
            P.op(eng, lambda e, o=out_ap, i=in_ap: e.tensor_copy(o, i),
                 reads=rd, writes=wr)

        rr = lambda t: t.ap().rearrange("(c p) o -> p c o", p=128)
        dma(wk_s[:], rr(wk), ["wk"], ["wk_s"])
        dma(wv_s[:], rr(wv), ["wv"], ["wv_s"])
        dma(id_s[:], ident_d[:, :], ["ident_d"], ["id_s"])
        xtr = xT.ap().rearrange("(c p) n -> p c n", p=128)
        dma(xf[0][:], xtr[:, :, 0:128], ["xT"], ["xf0"])
        dma(csb[0][:], cosr[:, 0:128], ["cosr"], ["csb0"])
        dma(snb[0][:], sinr[:, 0:128], ["sinr"], ["snb0"])
        P.op("vector", lambda e: e.memset(s0[:], 0.0), writes=["s0"])
        P.op("vector", lambda e: e.memset(s1[:], 0.0), writes=["s1"])
        P.op("vector", lambda e: e.memset(onesc[:], 1.0), writes=["onesc"])
        P.op("vector", lambda e: e.memset(ones1[:], 1.0), writes=["ones1"])
        P.op("vector", lambda e: e.memset(cvals[:, 0:1], EPS), writes=["cvals"])
        P.op("vector", lambda e: e.memset(cvals[:, 1:2], 1.0 / 1024.0), writes=["cvals"])

        def stats(src, srckey):
            tt("vector", sq[:], src[:], src[:], ALU.mult, [srckey], ["sq"])
            for c in range(8):
                mm(pE[0:1, 0:128], onesc[:], CH(src, c), [srckey, "onesc"], "pE",
                   start=(c == 0), stop=(c == 7))
            for c in range(8):
                mm(pE[0:1, 128:256], onesc[:], CH(sq, c), ["sq", "onesc"], "pE",
                   start=(c == 0), stop=(c == 7))
            act(mur[:], pE[0:1, 0:128], AF.Copy, ["pE", "cvals"], ["mur"], scale=1.0 / 1024.0)
            tt("vector", msq[:], mur[:], mur[:], ALU.mult, ["mur"], ["msq"])
            stt("vector", vrr[:], pE[0:1, 128:256], 1.0 / 1024.0, msq[:],
                ALU.mult, ALU.subtract, ["pE", "msq"], ["vrr"])
            act(rsd[:], vrr[:], AF.Sqrt, ["vrr", "cvals"], ["rsd"], bias=EPS)
            P.op("vector", lambda e: e.reciprocal(rsd[:], rsd[:]),
                 reads=["rsd"], writes=["rsd"], hazard=True)
            mm(pE[:, 256:384], ones1[:], mur[:], ["ones1", "mur"], "pE", True, True)
            mm(pE[:, 384:512], ones1[:], rsd[:], ["ones1", "rsd"], "pE", True, True)

        def rope(eng, srcb, srckey, dstb, dstkey, tmp, tmpkey, cosb, cosk, sinb, sink):
            for c in range(4):
                sA, sB = CH(srcb, c), CH(srcb, c + 4)
                tt(eng, CH(dstb, c), sA, cosb[:], ALU.mult, [srckey, cosk], [dstkey])
                tt(eng, tmp[:], sB, sinb[:], ALU.mult, [srckey, sink], [tmpkey])
                tt(eng, CH(dstb, c), CH(dstb, c), tmp[:], ALU.subtract,
                   [dstkey, tmpkey], [dstkey])
                tt(eng, CH(dstb, c + 4), sA, sinb[:], ALU.mult, [srckey, sink], [dstkey])
                tt(eng, tmp[:], sB, cosb[:], ALU.mult, [srckey, cosk], [tmpkey])
                tt(eng, CH(dstb, c + 4), CH(dstb, c + 4), tmp[:], ALU.add,
                   [dstkey, tmpkey], [dstkey])

        for i in range(NT):
            xb, cb, sb_ = xf[i % 2], csb[i % 2], snb[i % 2]
            kx, kc_, ks_ = f"xf{i % 2}", f"csb{i % 2}", f"snb{i % 2}"
            if i + 1 < NT:
                nb = (i + 1) % 2
                sl = slice((i + 1) * 128, (i + 2) * 128)
                dma(xf[nb][:], xtr[:, :, sl], ["xT"], [f"xf{nb}"])
                dma(csb[nb][:], cosr[:, sl], ["cosr"], [f"csb{nb}"])
                dma(snb[nb][:], sinr[:, sl], ["sinr"], [f"snb{nb}"])
            stats(xb, kx)
            for c in range(8):
                tt("vector", CH(u1, c), CH(xb, c), pE[:, 256:384], ALU.subtract,
                   [kx, "pE"], ["u1"])
                tt("vector", CH(u1, c), CH(u1, c), pE[:, 384:512], ALU.mult,
                   ["u1", "pE"], ["u1"])
            for oc in range(8):
                pq = pA if oc < 4 else pB
                col = (oc % 4) * 128
                for kcc in range(8):
                    mm(pq[:, col:col + 128], wk_s[:, kcc * E + oc * 128: kcc * E + oc * 128 + 128],
                       CH(u1, kcc), ["wk_s", "u1"], K(pq), start=(kcc == 0), stop=(kcc == 7))
            for oc in range(8):
                pq = pA if oc < 4 else pB
                col = (oc % 4) * 128
                act(CH(ks, oc), pq[:, col:col + 128], AF.Silu, [K(pq)], ["ks"])
            rope("gpsimd", ks, "ks", kr, "kr", rtg, "rtg", cb, kc_, sb_, ks_)
            for c in range(8):
                pt = pC if c < 4 else pD
                col = (c % 4) * 128
                transp(pt[:, col:col + 128], CH(kr, c), ["kr"], K(pt))
            vcopy("vector", ktk[:, 0:512], pC[:, :], ["pC"], ["ktk"])
            vcopy("vector", ktk[:, 512:1024], pD[:, :], ["pD"], ["ktk"])
            for kcc in range(8):
                mm(pC[:, :], CH(u1, kcc), wv_s[:, kcc * E: kcc * E + 512],
                   ["wv_s", "u1"], "pC", start=(kcc == 0), stop=(kcc == 7))
            for kcc in range(8):
                mm(pD[:, :], CH(u1, kcc), wv_s[:, kcc * E + 512: kcc * E + 1024],
                   ["wv_s", "u1"], "pD", start=(kcc == 0), stop=(kcc == 7))
            vcopy("vector", vtk[:, 0:512], pC[:, :], ["pC"], ["vtk"])
            vcopy("vector", vtk[:, 512:1024], pD[:, :], ["pD"], ["vtk"])
            for h in range(16):
                band = 32 * (h % 4)
                hs = h // 4
                for p_ in range(2):
                    lhs = ktk[:, 512 * p_ + 32 * h: 512 * p_ + 32 * h + 32]
                    rhs = vtk[:, 64 * h: 64 * h + 64]
                    mm(pH[band:band + 32, 256 * p_ + 64 * hs: 256 * p_ + 64 * hs + 64],
                       lhs, rhs, ["ktk", "vtk"], "pH", start=True, stop=True,
                       tile_position=(0, band))
            tt("vector", s0[:], s0[:], pH[:, 0:256], ALU.add, ["pH", "s0"], ["s0"])
            tt("vector", s1[:], s1[:], pH[:, 256:512], ALU.add, ["pH", "s1"], ["s1"])

        dma(s0_out[:, :], s0[:], ["s0"], ["s0_out"])
        dma(s1_out[:, :], s1[:], ["s1"], ["s1_out"])
        P.final_wait("sync")
        P.emit_into(blk)
    return nc


def build_l2():
    nc = bass.Bass("TRN2", target_bir_lowering=False)
    TOK = 4096

    y_h0 = nc.dram_tensor("y_h0", [4096, E], F32, kind="ExternalInput")
    y_h1 = nc.dram_tensor("y_h1", [4096, E], F32, kind="ExternalInput")
    y_mine = nc.dram_tensor("y_mine", [4096, E], F32, kind="ExternalInput")
    msel_all = nc.dram_tensor("msel_all", [4, 8192, 8], F32, kind="ExternalInput")
    msel_mine = nc.dram_tensor("msel_mine", [8192, 8], F32, kind="ExternalInput")
    w1d = nc.dram_tensor("w1d", [E, 8 * 256], F32, kind="ExternalInput")
    w2d = nc.dram_tensor("w2d", [256, 8 * E], F32, kind="ExternalInput")
    b1d = nc.dram_tensor("b1d", [128, 16], F32, kind="ExternalInput")
    b2d = nc.dram_tensor("b2d", [8, E], F32, kind="ExternalInput")
    pick8_d = nc.dram_tensor("pick8_d", [8, 8 * 128], F32, kind="ExternalInput")
    gamr = nc.dram_tensor("gamr", [1, E], F32, kind="ExternalInput")
    betr = nc.dram_tensor("betr", [1, E], F32, kind="ExternalInput")
    mh_in = nc.dram_tensor("mh_in", [128, 2], F32, kind="ExternalInput")
    fidx_in = nc.dram_tensor("fidx_in", [128, 1], U32, kind="ExternalInput")
    accb_in = nc.dram_tensor("accb_in", [128, 1], F32, kind="ExternalInput")
    ident_d = nc.dram_tensor("ident_d", [128, 128], F32, kind="ExternalInput")
    lte_d = nc.dram_tensor("lte_d", [128, 128], F32, kind="ExternalInput")
    e8_d = nc.dram_tensor("e8_d", [128, 8], F32, kind="ExternalInput")
    e8t_d = nc.dram_tensor("e8t_d", [8, 128], F32, kind="ExternalInput")
    ios_d = nc.dram_tensor("ios_d", [128, 512], F32, kind="ExternalInput")
    ebase_d = nc.dram_tensor("ebase_d", [128, 3], F32, kind="ExternalInput")

    out_d = nc.dram_tensor("out_d", [4096, E], F32, kind="ExternalOutput")

    u2fd = nc.dram_tensor("u2fd", [8192, E], F32)
    routeT = nc.dram_tensor("routeT", [8, 8192], F32)
    xgd = nc.dram_tensor("xgd", [8 * 4096 + 1, E], F32)
    coutd = nc.dram_tensor("coutd", [8 * 4097, E], F32)
    accbig = nc.dram_tensor("accbig", [2 * 8193 + 1, E], F32, addr_space="Shared")

    es = contextlib.ExitStack()
    with es:
        sbuf = lambda n, sh, dt=F32: es.enter_context(nc.sbuf_tensor(n, sh, dt))
        psum = lambda n: es.enter_context(nc.psum_tensor(n, [128, 512], F32))

        w1s = sbuf("w1s", [128, 8 * 2048])
        w2s = sbuf("w2s", [128, 2 * 8192])
        b1s = sbuf("b1s", [128, 16]); b2s = sbuf("b2s", [8, E])
        pick8 = sbuf("pick8", [8, 8 * 128])
        id_s = sbuf("id_s", [128, 128]); lte = sbuf("lte", [128, 128])
        e8 = sbuf("e8", [128, 8]); e8t = sbuf("e8t", [8, 128])
        ios = sbuf("ios", [128, 512])
        ebase = sbuf("ebase", [128, 3]); mh = sbuf("mh", [128, 2])
        fidx = sbuf("fidx", [128, 1], U32); accb = sbuf("accb", [128, 1])
        onesr = sbuf("onesr", [1, 512]); ones1 = sbuf("ones1", [1, 128])
        onef = sbuf("onef", [128, 16])
        gsrow = sbuf("gsrow", [1, E]); bsrow = sbuf("bsrow", [1, E])
        big0 = sbuf("big0", [128, 4096])
        big1 = sbuf("big1", [128, 4096])
        yt0 = sbuf("yt0", [128, E])
        xgt0 = sbuf("xgt0", [128, E])
        accsb0 = sbuf("accsb0", [128, E])
        r1 = sbuf("r1", [128, 1]); r2 = sbuf("r2", [128, 1])
        r3 = sbuf("r3", [128, 1]); r4 = sbuf("r4", [128, 1])
        mstile = sbuf("mstile", [128, 32]); rt = sbuf("rt", [128, 8])
        Ti = sbuf("Ti", [128, 1], I32); Tc = sbuf("Tc", [128, 1], I32)
        cnt8 = sbuf("cnt8", [8, 1]); cntb = sbuf("cntb", [128, 1])
        indc = sbuf("indc", [128, 1]); indi = sbuf("indi", [128, 1], I32)
        cntg = sbuf("cntg", [128, 1]); mfill = sbuf("mfill", [128, 1])
        offc = sbuf("offc", [128, 1])
        idxgT = sbuf("idxgT", [128, 512], U32)
        idxoT = sbuf("idxoT", [128, 512], U32)
        idxaT = sbuf("idxaT", [128, 512], U32)
        cvals = sbuf("cvals", [128, 2])
        flagrd = sbuf("flagrd", [2, 1], F32)
        R = big0[:, 0:512]; cmp_ = big0[:, 512:1024]
        eqt = big0[:, 1024:1536]; selg = big0[:, 1536:2048]
        pp_a = big0[:, 2048:2560]; pp_b = big0[:, 2560:3072]
        pose = big0[:, 3072:3584]; sel = big0[:, 3584:4096]
        pos = big1[:, 0:512]; scr = big1[:, 512:1024]
        scr2 = big1[:, 1024:1536]; idxg = big1[:, 1536:2048]
        idxo = big1[:, 2048:2560]; idxa = big1[:, 2560:3072]
        tps = big1[:, 3072:3584]
        rte = big1[0:8, 3584:4096]
        hbf = big1[:, 0:1024]; obf = big1[:, 1024:2048]
        gb = big1[:, 0:1024]; bb = big1[:, 1024:2048]
        u2t0 = big1[:, 2048:3072]; sqt = big1[:, 3072:4096]
        a01 = big0[:, 0:1024]; a10 = big0[:, 1024:2048]; a11 = big0[:, 2048:3072]
        xgf = big0
        zgv = big0[:, 0:1024]
        pA = psum("pA"); pB = psum("pB"); pC = psum("pC"); pD = psum("pD")
        pE = psum("pE")
        PK = {id(p): n for p, n in [(pA, "pA"), (pB, "pB"), (pC, "pC"),
                                    (pD, "pD"), (pE, "pE")]}
        K = lambda b: PK.get(id(b), getattr(b, "name", str(b)))

        P = Plan(nc, es)
        blk = es.enter_context(nc.Block())
        nc.const_aps.register(F32, EPS, cvals[:, 0:1])
        nc.const_aps.register(F32, 1.0 / 1024.0, cvals[:, 1:2])

        def dma(out_ap, in_ap, rd, wr, eng="sync"):
            P.op(eng, lambda e, o=out_ap, i=in_ap: e.dma_start(out=o, in_=i),
                 reads=rd, writes=wr, dma=True)

        def mm(out_ap, lhsT, rhs, rd, pbank, start, stop, tp=None):
            P.op("tensor",
                 lambda e, o=out_ap, l=lhsT, r=rhs, s=start, t=stop, q=tp:
                 e.matmul(o, l, r, start=s, stop=t, tile_position=q),
                 reads=rd, writes=[pbank])

        def transp(out_ap, in_ap, rd, pbank):
            P.op("tensor", lambda e, o=out_ap, i=in_ap: e.transpose(o, i, id_s[:]),
                 reads=list(rd) + ["id_s"], writes=[pbank])

        def tt(eng, out_ap, a, b, op, rd, wr):
            P.op(eng, lambda e, o=out_ap, x=a, y=b, p=op: e.tensor_tensor(o, x, y, p),
                 reads=rd, writes=wr)

        def ts(eng, out_ap, a, s1v, s2v, op0, op1, rd, wr):
            if s2v is None:
                P.op(eng, lambda e, o=out_ap, x=a, s=s1v, p=op0:
                     e.tensor_scalar(o, x, s, None, p), reads=rd, writes=wr)
            else:
                P.op(eng, lambda e, o=out_ap, x=a, s=s1v, s2=s2v, p=op0, p2=op1:
                     e.tensor_scalar(o, x, s, s2, p, p2), reads=rd, writes=wr)

        def stt(eng, out_ap, in0, scal, in1, op0, op1, rd, wr):
            P.op(eng, lambda e, o=out_ap, a=in0, s=scal, b=in1, p0=op0, p1=op1:
                 e.scalar_tensor_tensor(o, a, s, b, p0, p1), reads=rd, writes=wr)

        def act(out_ap, in_ap, func, rd, wr, bias=0.0, scale=1.0):
            P.op("scalar", lambda e, o=out_ap, i=in_ap, f=func, b=bias, s=scale:
                 e.activation(o, i, f, bias=b, scale=s), reads=rd, writes=wr)

        def vcopy(eng, out_ap, in_ap, rd, wr):
            P.op(eng, lambda e, o=out_ap, i=in_ap: e.tensor_copy(o, i),
                 reads=rd, writes=wr)

        def red(out_ap, in_ap, op, rd, wr):
            P.op("vector", lambda e, o=out_ap, i=in_ap, p=op:
                 e.tensor_reduce(o, i, axis=AX.X, op=p), reads=rd, writes=wr,
                 hazard=True)

        dma(w1s[:], w1d.ap().rearrange("(c p) o -> p c o", p=128), ["w1d"], ["w1s"])
        dma(w2s[:], w2d.ap().rearrange("(c p) o -> p c o", p=128), ["w2d"], ["w2s"])
        dma(b1s[:], b1d[:, :], ["b1d"], ["b1s"])
        dma(b2s[:], b2d[:, :], ["b2d"], ["b2s"])
        dma(pick8[:], pick8_d[:, :], ["pick8_d"], ["pick8"])
        dma(id_s[:], ident_d[:, :], ["ident_d"], ["id_s"])
        dma(lte[:], lte_d[:, :], ["lte_d"], ["lte"])
        dma(e8[:], e8_d[:, :], ["e8_d"], ["e8"])
        dma(e8t[:], e8t_d[:, :], ["e8t_d"], ["e8t"])
        dma(ios[:], ios_d[:, :], ["ios_d"], ["ios"])
        dma(ebase[:], ebase_d[:, :], ["ebase_d"], ["ebase"])
        dma(mh[:], mh_in[:, :], ["mh_in"], ["mh"])
        dma(fidx[:], fidx_in[:, :], ["fidx_in"], ["fidx"])
        dma(accb[:], accb_in[:, :], ["accb_in"], ["accb"])
        dma(gsrow[:], gamr[:, :], ["gamr"], ["gsrow"])
        dma(bsrow[:], betr[:, :], ["betr"], ["bsrow"])
        P.op("vector", lambda e: e.memset(onesr[:], 1.0), writes=["onesr"])
        P.op("vector", lambda e: e.memset(ones1[:], 1.0), writes=["ones1"])
        P.op("vector", lambda e: e.memset(onef[:], 1.0), writes=["onef"])
        P.op("vector", lambda e: e.memset(cvals[:, 0:1], EPS), writes=["cvals"])
        P.op("vector", lambda e: e.memset(cvals[:, 1:2], 1.0 / 1024.0), writes=["cvals"])
        P.op("vector", lambda e: e.memset(flagrd[:], 0.0), writes=["flagrd"])
        P.op("vector", lambda e: e.memset(yt0[:], 0.0), writes=["yt0"])
        for e_ in range(8):
            dma(coutd[e_ * 4097 + 4096: e_ * 4097 + 4097, :], yt0[0:1, :],
                ["yt0"], ["coutd"])

        def ln_token(ybuf, ykey, u2ap, u2key):
            act(sqt, ybuf[:], AF.Square, [ykey], ["big1"])
            red(r1[:], ybuf[:], ALU.add, [ykey], ["r1"])
            red(r2[:], sqt, ALU.add, ["big1"], ["r2"])
            ts("vector", r1[:], r1[:], 1.0 / 1024.0, None, ALU.mult, None, ["r1"], ["r1"])
            tt("vector", r3[:], r1[:], r1[:], ALU.mult, ["r1"], ["r3"])
            stt("vector", r2[:], r2[:], 1.0 / 1024.0, r3[:], ALU.mult, ALU.subtract,
                ["r2", "r3"], ["r2"])
            act(r2[:], r2[:], AF.Sqrt, ["r2", "cvals"], ["r2"], bias=EPS)
            P.op("vector", lambda e: e.reciprocal(r2[:], r2[:]),
                 reads=["r2"], writes=["r2"], hazard=True)
            tt("vector", r3[:], r1[:], r2[:], ALU.mult, ["r1", "r2"], ["r3"])
            ts("vector", r3[:], r3[:], -1.0, None, ALU.mult, None, ["r3"], ["r3"])
            ts("vector", u2ap, ybuf[:], r2[:, 0:1], r3[:, 0:1], ALU.mult, ALU.add,
               [ykey, "r2", "r3"], [u2key])

        for half in range(2):
            ysrc = y_h0 if half == 0 else y_h1
            yk = "y_h0" if half == 0 else "y_h1"
            for j in range(32):
                dma(yt0[:], ysrc[j * 128:(j + 1) * 128, :], [yk], ["yt0"])
                ln_token(yt0, "yt0", u2t0, "big1")
                dma(u2fd[half * 4096 + j * 128: half * 4096 + (j + 1) * 128, :],
                    u2t0, ["big1"], ["u2fd"])

        for j in range(64):
            dma(mstile[:],
                msel_all.ap()[:, j * 128:(j + 1) * 128, :].rearrange("b p e -> p b e"),
                ["msel_all"], ["mstile"])
            tt("vector", rt[:], mstile[:, 0:8], mstile[:, 8:16], ALU.add,
               ["mstile"], ["rt"])
            tt("vector", rt[:], rt[:], mstile[:, 16:24], ALU.add, ["mstile", "rt"], ["rt"])
            tt("vector", rt[:], rt[:], mstile[:, 24:32], ALU.add, ["mstile", "rt"], ["rt"])
            ts("vector", rt[:], rt[:], 1e-6, None, ALU.add, None, ["rt"], ["rt"])
            P.op("vector", lambda e: e.reciprocal(rt[:], rt[:]),
                 reads=["rt"], writes=["rt"], hazard=True)
            dma(mstile[:, 0:8], msel_mine[j * 128:(j + 1) * 128, :],
                ["msel_mine"], ["mstile"], eng="scalar")
            tt("vector", rt[:], rt[:], mstile[:, 0:8], ALU.mult, ["mstile", "rt"], ["rt"])
            transp(pC[0:8, (j % 4) * 128:(j % 4) * 128 + 128], rt[:, 0:8], ["rt"], "pC")
            if j % 4 == 3:
                vcopy("vector", rte, pC[0:8, :], ["pC"], ["big1"])
                dma(routeT[:, (j - 3) * 128: (j + 1) * 128], rte,
                    ["big1"], ["routeT"])

        for e_ in range(8):
            dma(R[16 * e_:16 * e_ + 16, :],
                routeT.ap()[e_:e_ + 1, :].rearrange("o (s f) -> (o s) f", f=512),
                ["routeT"], ["big0"])
        P.op("vector", lambda e: e.memset(Ti[:], 0), writes=["Ti"])

        def count_ge(thr_ap, dst, dkey, rdextra):
            ts("vector", cmp_, R, thr_ap.bitcast(F32), None, ALU.is_ge,
               None, ["big0"] + rdextra, ["big0"])
            red(r4[:], cmp_, ALU.add, ["big0"], ["r4"])
            mm(pE[0:8, 0:1], e8[:], r4[:], ["e8", "r4"], "pE", True, True)
            vcopy("vector", cnt8[:], pE[0:8, 0:1], ["pE"], ["cnt8"])
            mm(pE[:, 4:5], e8t[:], cnt8[:], ["e8t", "cnt8"], "pE", True, True)
            vcopy("vector", dst[:], pE[:, 4:5], ["pE"], [dkey])

        for bit in range(29, -1, -1):
            P.op("vector", lambda e, b=bit: e.tensor_single_scalar(
                Tc[:], Ti[:], 1 << b, ALU.add), reads=["Ti"], writes=["Tc"])
            count_ge(Tc[:, 0:1], cntb, "cntb", ["Tc"])
            ts("vector", indc[:], cntb[:], 4096.0, None, ALU.is_ge, None,
               ["cntb"], ["indc"])
            ts("vector", indc[:], indc[:], float(1 << bit), None, ALU.mult, None,
               ["indc"], ["indc"])
            vcopy("vector", indi[:], indc[:], ["indc"], ["indi"])
            tt("vector", Ti[:], Ti[:], indi[:], ALU.add, ["Ti", "indi"], ["Ti"])
        P.op("vector", lambda e: e.tensor_single_scalar(Tc[:], Ti[:], 1, ALU.add),
             reads=["Ti"], writes=["Tc"])
        count_ge(Tc[:, 0:1], cntg, "cntg", ["Tc"])
        ts("vector", mfill[:], cntg[:], -1.0, None, ALU.mult, None, ["cntg"], ["mfill"])
        ts("vector", mfill[:], mfill[:], 4096.0, None, ALU.add, None, ["mfill"], ["mfill"])
        ts("vector", selg, R, Tc[:, 0:1].bitcast(F32), None, ALU.is_ge, None,
           ["big0", "Tc"], ["big0"])
        ts("vector", eqt, R, Ti[:, 0:1].bitcast(F32), None, ALU.is_equal, None,
           ["big0", "Ti"], ["big0"])

        def prefix_excl(srcap, dstap):
            vcopy("vector", pp_a, srcap, ["big0"], ["big0"])
            cur, oth = pp_a, pp_b
            d = 1
            while d < 512:
                vcopy("vector", oth[:, 0:d], cur[:, 0:d], ["big0"], ["big0"])
                tt("vector", oth[:, d:512], cur[:, d:512], cur[:, 0:512 - d],
                   ALU.add, ["big0"], ["big0"])
                cur, oth = oth, cur
                d *= 2
            mm(pE[:, 8:9], lte[:], cur[:, 511:512], ["lte", "big0"], "pE", True, True)
            vcopy("vector", offc[:], pE[:, 8:9], ["pE"], ["offc"])
            ts("vector", cur[:], cur[:], offc[:, 0:1], None, ALU.add, None,
               ["big0", "offc"], ["big0"])
            tt("vector", dstap, cur[:], srcap, ALU.subtract, ["big0"],
               ["big0" if dstap is not pos else "big1"])

        prefix_excl(eqt, pose)
        ts("vector", scr, pose, mfill[:, 0:1], None, ALU.is_lt, None,
           ["big0", "mfill"], ["big1"])
        tt("vector", scr, scr, eqt, ALU.mult, ["big1", "big0"], ["big1"])
        tt("vector", sel, selg, scr, ALU.add, ["big0", "big1"], ["big0"])
        prefix_excl(sel, pos)

        ts("vector", scr, pos, ebase[:, 0:1], None, ALU.add, None,
           ["big1", "ebase"], ["big1"])
        tt("vector", scr, scr, sel, ALU.mult, ["big1", "big0"], ["big1"])
        ts("vector", scr2, sel, -BIG, None, ALU.mult, None, ["big0"], ["big1"])
        ts("vector", scr2, scr2, BIG, None, ALU.add, None, ["big1"], ["big1"])
        tt("vector", idxg, scr, scr2, ALU.add, ["big1"], ["big1"])
        tt("vector", scr, pos, sel, ALU.mult, ["big1", "big0"], ["big1"])
        ts("vector", scr2, sel, -4096.0, None, ALU.mult, None, ["big0"], ["big1"])
        ts("vector", scr2, scr2, 4096.0, None, ALU.add, None, ["big1"], ["big1"])
        tt("vector", scr, scr, scr2, ALU.add, ["big1"], ["big1"])
        ts("vector", idxo, scr, ebase[:, 1:2], None, ALU.add, None,
           ["big1", "ebase"], ["big1"])
        ts("vector", idxa, ios[:], accb[:, 0:1], None, ALU.add, None,
           ["ios", "accb"], ["big1"])
        for srcap, dstb, dstkey in [(idxg, idxgT, "idxgT"), (idxo, idxoT, "idxoT"),
                                    (idxa, idxaT, "idxaT")]:
            for k4 in range(4):
                transp(pD[:, k4 * 128:k4 * 128 + 128],
                       srcap[:, k4 * 128:(k4 + 1) * 128], ["big1"], "pD")
            vcopy("vector", tps, pD[:, :], ["pD"], ["big1"])
            vcopy("vector", dstb[:], tps, ["big1"], [dstkey])

        P.barrier()

        for j in range(64):
            dma(yt0[:], u2fd[j * 128:(j + 1) * 128, :], ["u2fd"], ["yt0"],
                eng="scalar")
            kq, pq = j % 4, j // 4
            for e_ in range(8):
                col = kq * 128 + 16 * e_ + pq
                P.op("gpsimd",
                     lambda e, col=col: e.indirect_dma_start(
                         out=xgd[:, :],
                         out_offset=IndirectOffsetOnAxis(idxgT[:, col:col + 1], 0),
                         in_=yt0[:], in_offset=None),
                     reads=["yt0", "idxgT"], writes=["xgd"], dma=True)

        for e_ in range(8):
            for g in range(8):
                for c in range(4):
                    rows = e_ * 4096 + g * 512 + c * 128
                    dma(xgt0[:], xgd[rows:rows + 128, :], ["xgd"], ["xgt0"])
                    for kc in range(8):
                        pt = pA if kc < 4 else pB
                        transp(pt[:, (kc % 4) * 128:(kc % 4) * 128 + 128],
                               xgt0[:, kc * 128:(kc + 1) * 128], ["xgt0"], K(pt))
                    P.op("vector", lambda e, c=c: e.tensor_copy(
                        xgf[:].rearrange("p (k t) -> p k t", k=8)[:, 0:4, c * 128:(c + 1) * 128],
                        pA[:].rearrange("p (k t) -> p k t", k=4)),
                        reads=["pA"], writes=["big0"])
                    P.op("vector", lambda e, c=c: e.tensor_copy(
                        xgf[:].rearrange("p (k t) -> p k t", k=8)[:, 4:8, c * 128:(c + 1) * 128],
                        pB[:].rearrange("p (k t) -> p k t", k=4)),
                        reads=["pB"], writes=["big0"])
                for hc in range(2):
                    pq = pC if hc == 0 else pD
                    for kc in range(8):
                        mm(pq[:, :], w1s[:, kc * 2048 + e_ * 256 + hc * 128:
                                         kc * 2048 + e_ * 256 + hc * 128 + 128],
                           xgf[:, kc * 512:(kc + 1) * 512],
                           ["w1s", "big0"], K(pq), start=(kc == 0), stop=(kc == 7))
                    act(hbf[:, hc * 512:(hc + 1) * 512], pq[:, :], AF.Gelu,
                        [K(pq), "b1s"], ["big1"],
                        bias=b1s[:, e_ * 2 + hc: e_ * 2 + hc + 1])
                for c in range(4):
                    for half in range(2):
                        pq = pC if half == 0 else pD
                        for hc in range(2):
                            mm(pq[:, :], hbf[:, hc * 512 + c * 128: hc * 512 + c * 128 + 128],
                               w2s[:, hc * 8192 + e_ * 1024 + half * 512:
                                   hc * 8192 + e_ * 1024 + half * 512 + 512],
                               ["big1", "w2s"], K(pq), start=(hc == 0), stop=False)
                        mm(pq[:, :], pick8[:, e_ * 128:(e_ + 1) * 128],
                           b2s[0:8, half * 512: half * 512 + 512],
                           ["pick8", "b2s"], K(pq), start=False, stop=True)
                        if half == 0:
                            vcopy("vector", obf[:, 0:512], pq[:, :], [K(pq)], ["big1"])
                        else:
                            act(obf[:, 512:1024], pq[:, :], AF.Copy, [K(pq)], ["big1"])
                    rows = e_ * 4097 + g * 512 + c * 128
                    dma(coutd[rows:rows + 128, :], obf, ["big1"], ["coutd"])

        for j in range(64):
            kq, pq = j % 4, j // 4
            for e_ in range(8):
                col = kq * 128 + 16 * e_ + pq
                P.op("gpsimd",
                     lambda e, col=col, first=(e_ == 0): e.indirect_dma_start(
                         out=accsb0[:], out_offset=None,
                         in_=coutd[:, :],
                         in_offset=IndirectOffsetOnAxis(idxoT[:, col:col + 1], 0),
                         compute_op=(ALU.bypass if first else ALU.add)),
                     reads=["coutd", "idxoT"], writes=["accsb0"], dma=True)
            acol = kq * 128 + pq
            P.op("gpsimd",
                 lambda e, acol=acol: e.indirect_dma_start(
                     out=accbig[:, :],
                     out_offset=IndirectOffsetOnAxis(idxaT[:, acol:acol + 1], 0),
                     in_=accsb0[:], in_offset=None),
                 reads=["accsb0", "idxaT"], writes=["accbig"], dma=True)

        P.op("gpsimd", lambda e: e.indirect_dma_start(
            out=accbig[:, :], out_offset=IndirectOffsetOnAxis(fidx[:, 0:1], 0),
            in_=onef[:, :], in_offset=None),
            reads=["onef", "fidx", "accbig"], writes=["accbig"], dma=True)

        psem = es.enter_context(nc.semaphore("psem"))

        def poll2(gpsimd):
            with gpsimd.register("r0") as rr0, gpsimd.register("r1") as rr1, \
                 gpsimd.register("rnot") as rnot, gpsimd.register("rcnt") as rcnt:
                gpsimd.reg_mov(rnot, 1)
                gpsimd.reg_mov(rcnt, 0)
                with gpsimd.While(rnot):
                    gpsimd.nop(cycle_cnt=20000)
                    gpsimd.dma_start(out=flagrd[0:1, :], in_=accbig[8192:8193, 0:1]).then_inc(psem, 16)
                    gpsimd.dma_start(out=flagrd[1:2, :], in_=accbig[8193 + 8192:8193 + 8193, 0:1]).then_inc(psem, 16)
                    gpsimd.reg_add(rcnt, rcnt, 32)
                    gpsimd.wait_ge(psem, rcnt)
                    gpsimd.reg_load(rr0, flagrd[0:1, 0:1].bitcast(I32))
                    gpsimd.reg_load(rr1, flagrd[1:2, 0:1].bitcast(I32))
                    gpsimd.reg_add(rnot, rr0, rr1)
                    gpsimd.reg_sub(rnot, 2 * 0x3F800000, rnot)
            return gpsimd.nop()

        P.op("gpsimd", poll2, reads=["accbig", "flagrd"], writes=["polldone", "flagrd"])
        P.barrier()

        for half in range(2):
            mm(pA[:, :], ones1[:], gsrow[0:1, half * 512:(half + 1) * 512],
               ["ones1", "gsrow"], "pA", True, True)
            vcopy("vector", gb[:, half * 512:(half + 1) * 512], pA[:, :], ["pA"], ["big1"])
            mm(pB[:, :], ones1[:], bsrow[0:1, half * 512:(half + 1) * 512],
               ["ones1", "bsrow"], "pB", True, True)
            vcopy("vector", bb[:, half * 512:(half + 1) * 512], pB[:, :], ["pB"], ["big1"])

        def ln_stats_inline(srcap, srckey):
            act(sqt, srcap, AF.Square, [srckey], ["big1"])
            red(r1[:], srcap, ALU.add, [srckey], ["r1"])
            red(r2[:], sqt, ALU.add, ["big1"], ["r2"])
            ts("vector", r1[:], r1[:], 1.0 / 1024.0, None, ALU.mult, None, ["r1"], ["r1"])
            tt("vector", r3[:], r1[:], r1[:], ALU.mult, ["r1"], ["r3"])
            stt("vector", r2[:], r2[:], 1.0 / 1024.0, r3[:], ALU.mult, ALU.subtract,
                ["r2", "r3"], ["r2"])
            act(r2[:], r2[:], AF.Sqrt, ["r2", "cvals"], ["r2"], bias=EPS)
            P.op("vector", lambda e: e.reciprocal(r2[:], r2[:]),
                 reads=["r2"], writes=["r2"], hazard=True)
            tt("vector", r3[:], r1[:], r2[:], ALU.mult, ["r1", "r2"], ["r3"])
            ts("vector", r3[:], r3[:], -1.0, None, ALU.mult, None, ["r3"], ["r3"])

        for j in range(32):
            dma(accsb0[:], accbig[j * 128:(j + 1) * 128, :],
                ["accbig", "polldone"], ["accsb0"])
            dma(a01, accbig[4096 + j * 128: 4096 + (j + 1) * 128, :],
                ["accbig", "polldone"], ["big0"], eng="scalar")
            dma(a10, accbig[8193 + j * 128: 8193 + (j + 1) * 128, :],
                ["accbig", "polldone"], ["big0"], eng="gpsimd")
            dma(a11, accbig[8193 + 4096 + j * 128: 8193 + 4096 + (j + 1) * 128, :],
                ["accbig", "polldone"], ["big0"], eng="gpsimd")
            dma(yt0[:], y_mine[j * 128:(j + 1) * 128, :], ["y_mine"], ["yt0"])
            ln_token(yt0, "yt0", u2t0, "big1")
            tt("vector", accsb0[:], accsb0[:], a10, ALU.add, ["accsb0", "big0"], ["accsb0"])
            tt("vector", a01, a01, a11, ALU.add, ["big0"], ["big0"])
            stt("vector", accsb0[:], accsb0[:], mh[:, 0:1], u2t0, ALU.mult, ALU.add,
                ["accsb0", "mh", "big1"], ["accsb0"])
            stt("vector", accsb0[:], a01, mh[:, 1:2], accsb0[:], ALU.mult, ALU.add,
                ["big0", "mh", "accsb0"], ["accsb0"])
            ln_stats_inline(accsb0[:], "accsb0")
            ts("vector", zgv, accsb0[:], r2[:, 0:1], r3[:, 0:1], ALU.mult, ALU.add,
               ["accsb0", "r2", "r3"], ["big0"])
            tt("vector", zgv, zgv, gb, ALU.mult, ["big0", "big1"], ["big0"])
            tt("vector", zgv, zgv, bb, ALU.add, ["big0", "big1"], ["big0"])
            ln_stats_inline(zgv, "big0")
            ts("vector", zgv, zgv, r2[:, 0:1], r3[:, 0:1], ALU.mult, ALU.add,
               ["big0", "r2", "r3"], ["big0"])
            tt("vector", zgv, zgv, yt0[:], ALU.add, ["big0", "yt0"], ["big0"])
            dma(out_d[j * 128:(j + 1) * 128, :], zgv, ["big0"], ["out_d"])

        P.final_wait("sync")
        P.emit_into(blk)
    return nc



def _l1_inputs(x_slice_T, Wq, Wk, Wv, Wo, Wg1, Wg2, gamma_a, beta_a, wg, bg, pos0, TOK):
    perm = np.zeros(1024, np.int64)
    for half in range(2):
        for h in range(16):
            for j in range(32):
                perm[half * 512 + h * 32 + j] = h * 64 + half * 32 + j
    inv = (np.float32(10000.0) ** (-(np.arange(32, dtype=np.float32) * np.float32(2.0)) / np.float32(64.0))).astype(np.float32)
    pos = (np.float32(pos0) + np.arange(TOK, dtype=np.float32)).astype(np.float32)
    ang = pos[:, None] * inv[None, :]
    cos_rep = np.ascontiguousarray(np.tile(np.cos(ang).astype(np.float32).T, (4, 1)))
    sin_rep = np.ascontiguousarray(np.tile(np.sin(ang).astype(np.float32).T, (4, 1)))
    return {
        "xT": x_slice_T,
        "wq": np.ascontiguousarray(Wq[:, perm]),
        "wk": np.ascontiguousarray(Wk[:, perm]),
        "wv": np.ascontiguousarray(Wv),
        "wo": np.ascontiguousarray(Wo),
        "wg1": np.ascontiguousarray(Wg1),
        "wg2": np.ascontiguousarray(Wg2),
        "gam": np.ascontiguousarray(gamma_a.reshape(8, 128).T).astype(np.float32),
        "bet": np.ascontiguousarray(beta_a.reshape(8, 128).T).astype(np.float32),
        "wgr": np.ascontiguousarray(wg),
        "bgr": bg.reshape(1, 16).astype(np.float32),
        "cosr": cos_rep, "sinr": sin_rep,
        "triu_d": np.triu(np.ones((128, 128), np.float32)),
        "ident_d": np.eye(128, dtype=np.float32),
        "iota16_d": np.tile(np.arange(16, dtype=np.float32)[None, :], (128, 1)),
    }


def _l2_inputs(parity, yb, m_all_sel, m_mine_sel, y_mine, w1, w2, b1, b2, gamma, beta):
    p = np.arange(128)
    ebase = np.stack([(p // 16) * 4096.0, (p // 16) * 4097.0, np.zeros(128)], 1)
    ios = ((p % 16) * 512)[:, None] + np.arange(512)[None, :]
    fidx = np.full((128, 1), 2 * 8193, np.uint32)
    fidx[0, 0] = parity * 8193 + 8192
    lte = np.zeros((128, 128), np.float32)
    for q in range(128):
        g0 = (q // 16) * 16
        lte[g0:q, q] = 1.0
    e8 = np.zeros((128, 8), np.float32)
    e8[p, p // 16] = 1.0
    mh = np.zeros((128, 2), np.float32)
    mh[:, parity] = 1.0
    w1d = np.ascontiguousarray(np.stack([w1[e] for e in range(8)], axis=1).reshape(E, 8 * 256))
    w2d = np.ascontiguousarray(np.stack([w2[e] for e in range(8)], axis=1).reshape(256, 8 * E))
    b1d = np.zeros((128, 16), np.float32)
    for e in range(8):
        for hc in range(2):
            b1d[:, e * 2 + hc] = b1[e][hc * 128:(hc + 1) * 128]
    b2d = np.stack([b2[e] for e in range(8)]).astype(np.float32)
    pick8 = np.zeros((8, 8 * 128), np.float32)
    for e in range(8):
        pick8[e, e * 128:(e + 1) * 128] = 1.0
    return {
        "y_h0": np.ascontiguousarray(yb[:4096]),
        "y_h1": np.ascontiguousarray(yb[4096:]),
        "y_mine": np.ascontiguousarray(y_mine),
        "msel_all": np.ascontiguousarray(m_all_sel).astype(np.float32),
        "msel_mine": np.ascontiguousarray(m_mine_sel).astype(np.float32),
        "w1d": w1d.astype(np.float32), "w2d": w2d.astype(np.float32),
        "b1d": b1d, "b2d": b2d, "pick8_d": pick8,
        "gamr": gamma.reshape(1, E).astype(np.float32),
        "betr": beta.reshape(1, E).astype(np.float32),
        "mh_in": mh, "fidx_in": fidx,
        "accb_in": np.full((128, 1), parity * 8193.0, np.float32),
        "ident_d": np.eye(128, dtype=np.float32),
        "lte_d": lte, "e8_d": e8,
        "e8t_d": np.ascontiguousarray(e8.T[:8]).astype(np.float32),
        "ios_d": ios.astype(np.float32),
        "ebase_d": ebase.astype(np.float32),
    }


def kernel(L0_attn_x, L0_attn_Wq, L0_attn_Wk, L0_attn_Wv, L0_attn_Wo,
           L0_attn_Wg1, L0_attn_Wg2, L0_attn_gamma, L0_attn_beta,
           L0_ffn_wg, L0_ffn_bg, L0_ffn_w1, L0_ffn_b1, L0_ffn_w2, L0_ffn_b2,
           L0_ffn_gamma, L0_ffn_beta):
    x = np.asarray(L0_attn_x, np.float32)
    B, S, _ = x.shape
    TOK = 4096
    NT = TOK // 128

    nc0 = build_l1a(NT=NT)
    in_maps0 = []
    for c in range(8):
        b, h = c // 2, c % 2
        xs = np.ascontiguousarray(x[b, h * TOK:(h + 1) * TOK, :].T)
        m1 = _l1_inputs(
            xs, L0_attn_Wq, L0_attn_Wk, L0_attn_Wv, L0_attn_Wo,
            L0_attn_Wg1, L0_attn_Wg2,
            np.asarray(L0_attn_gamma, np.float32), np.asarray(L0_attn_beta, np.float32),
            np.asarray(L0_ffn_wg, np.float32), np.asarray(L0_ffn_bg, np.float32),
            h * TOK, TOK)
        in_maps0.append({k: m1[k] for k in
                         ("xT", "wk", "wv", "cosr", "sinr", "ident_d")})
    res0 = run_bass_kernel_spmd(nc0, in_maps0, core_ids=list(range(8)))

    nc1 = build_l1(NT=NT)
    in_maps1 = []
    for c in range(8):
        b, h = c // 2, c % 2
        xs = np.ascontiguousarray(x[b, h * TOK:(h + 1) * TOK, :].T)
        m = _l1_inputs(
            xs, L0_attn_Wq, L0_attn_Wk, L0_attn_Wv, L0_attn_Wo,
            L0_attn_Wg1, L0_attn_Wg2,
            np.asarray(L0_attn_gamma, np.float32), np.asarray(L0_attn_beta, np.float32),
            np.asarray(L0_ffn_wg, np.float32), np.asarray(L0_ffn_bg, np.float32),
            h * TOK, TOK)
        if h == 1:
            m["s0_in"] = np.asarray(res0.results[2 * b]["s0_out"])
            m["s1_in"] = np.asarray(res0.results[2 * b]["s1_out"])
        else:
            m["s0_in"] = np.zeros((128, 256), np.float32)
            m["s1_in"] = np.zeros((128, 256), np.float32)
        in_maps1.append(m)
    res1 = run_bass_kernel_spmd(nc1, in_maps1, core_ids=list(range(8)))
    ys = [np.asarray(res1.results[c]["y_out"]) for c in range(8)]
    ms = [np.asarray(res1.results[c]["m_out"]) for c in range(8)]

    masked = np.stack([np.concatenate([ms[2 * b], ms[2 * b + 1]], 0) for b in range(4)])
    ybatch = [np.concatenate([ys[2 * b], ys[2 * b + 1]], 0) for b in range(4)]

    w1 = np.asarray(L0_ffn_w1, np.float32)
    w2 = np.asarray(L0_ffn_w2, np.float32)
    b1 = np.asarray(L0_ffn_b1, np.float32)
    b2 = np.asarray(L0_ffn_b2, np.float32)

    nc2 = build_l2()
    in_maps2 = []
    for c in range(8):
        b, h = c // 2, c % 2
        cols = slice(8 * h, 8 * h + 8)
        in_maps2.append(_l2_inputs(
            h, ybatch[b], masked[:, :, cols], masked[b][:, cols], ys[c],
            w1[cols], w2[cols], b1[cols], b2[cols],
            np.asarray(L0_ffn_gamma, np.float32), np.asarray(L0_ffn_beta, np.float32)))
    res2 = run_bass_kernel_spmd(nc2, in_maps2, core_ids=list(range(8)))

    out = np.empty((B, S, E), np.float32)
    for c in range(8):
        b, h = c // 2, c % 2
        out[b, h * TOK:(h + 1) * TOK, :] = np.asarray(res2.results[c]["out_d"])
    return out
